# version 01
# baseline (speedup 1.0000x reference)
import math
import numpy as np
import ml_dtypes
import concourse.bass as bass
import concourse.mybir as mybir
from concourse.bass_utils import run_bass_kernel_spmd

F32 = mybir.dt.float32
BF16 = mybir.dt.bfloat16
AF = mybir.ActivationFunctionType
OP = mybir.AluOpType
AX = mybir.AxisListType
BFNP = ml_dtypes.bfloat16

N_CORES = 8
D = 1024
H = 4
DK = 128
DV = 256
C = 128
TOK = 2048
NCHUNK = TOK // C
G512 = TOK // 512
NE = 18
E = 512
SCALE = float(DK) ** -0.5
NBLK = 6
BLKE = 3


class Sched:

    ENGINES = ("pe", "act", "dve", "pool", "sp")

    def __init__(self, nc):
        self.nc = nc
        self.recs = {e: [] for e in self.ENGINES}
        self.mem = {}
        self.dma_seq = {}
        self.referenced = {e: set() for e in self.ENGINES}
        self.psum_ids = set()

    def _box(self, ap):
        t = ap.tensor
        if id(t) in self.psum_ids:
            return (id(t), 0, 1 << 30, 0, 1 << 30)
        shape = t.shape
        row = 1
        for s in shape[1:]:
            row *= s
        off = ap.offset
        p_lo, f_lo = divmod(off, row) if row > 0 else (0, off)
        p_ext = 0
        f_ext = 0
        for step, cnt in ap.ap:
            if cnt <= 1 or step == 0:
                continue
            s = abs(step)
            if row > 1 and s % row == 0:
                p_ext += (s // row) * (cnt - 1)
            else:
                f_ext += s * (cnt - 1)
        return (id(t), p_lo, p_lo + p_ext + 1, f_lo, f_lo + f_ext + 1)

    @staticmethod
    def _olap(a, b):
        return (a[1] < b[2] and b[1] < a[2] and a[3] < b[4] and b[3] < a[4])

    def _deps(self, reads, writes):
        deps = set()
        for ap in reads:
            b = self._box(ap)
            for (ob, w, tok) in self.mem.get(b[0], ()):
                if w and self._olap(b, ob):
                    deps.add(tok)
        for ap in writes:
            b = self._box(ap)
            for (ob, w, tok) in self.mem.get(b[0], ()):
                if self._olap(b, ob):
                    deps.add(tok)
        return deps

    def op(self, engine, emit, reads=(), writes=(), dma_sem=None, tag="",
           extra_deps=(), force_inc=False):
        reads = list(reads)
        writes = list(writes)
        deps = self._deps(reads, writes) | set(extra_deps)
        if dma_sem is not None:
            RING = 8
            rr = getattr(self, "dma_rr", {})
            self.dma_rr = rr
            r = (rr.get(engine, -1) + 1) % RING
            rr[engine] = r
            slot = f"{engine}{r}"
            k = self.dma_seq.get(slot, 0) + 1
            self.dma_seq[slot] = k
            if k > 1:
                deps.add(("dma", slot, k - 1))
            tok = ("dma", slot, k)
        elif False:
            pass
        else:
            tok = ("eng", engine, len(self.recs[engine]))
        self.recs[engine].append(
            dict(emit=emit, deps=deps, tok=tok, tag=tag, force_inc=force_inc))
        for ap in reads:
            b = self._box(ap)
            self.mem.setdefault(b[0], []).append((b, False, tok))
        for ap in writes:
            b = self._box(ap)
            self.mem.setdefault(b[0], []).append((b, True, tok))
        return tok

    def barrier(self, tag="bar"):
        snap = set()
        for e in self.ENGINES:
            for rec in reversed(self.recs[e]):
                if rec["tok"] is not None and rec["tok"][0] == "eng":
                    snap.add(rec["tok"])
                    break
        for name, cnt in self.dma_seq.items():
            snap.add(("dma", name, cnt))
        for e in self.ENGINES:
            self.recs[e].append(
                dict(emit=None, deps=set(snap), tok=None, tag=tag,
                     force_inc=False))
        self.mem = {}

    def finalize(self):
        for e in self.ENGINES:
            for rec in self.recs[e]:
                for (kind, name, seq) in rec["deps"]:
                    if kind == "eng":
                        self.referenced[name].add(seq)
        inc_seqs = {}
        for e in self.ENGINES:
            s = set(self.referenced[e])
            for i, rec in enumerate(self.recs[e]):
                if rec.get("force_inc") and rec["emit"] is not None:
                    s.add(i)
            for i in range(len(self.recs[e]) - 1, -1, -1):
                if self.recs[e][i]["emit"] is not None:
                    s.add(i)
                    break
            inc_seqs[e] = sorted(s)
        self.inc_seqs = inc_seqs
        self.inc_index = {
            e: {seq: i + 1 for i, seq in enumerate(inc_seqs[e])}
            for e in self.ENGINES}

    def wait_value(self, tok):
        kind, name, seq = tok
        if kind == "dma":
            return ("sd_" + name, 16 * seq)
        lst = self.inc_seqs[name]
        import bisect
        i = bisect.bisect_left(lst, seq)
        assert i < len(lst), (name, seq, lst[-5:])
        return ("se_" + name, i + 1)

    def sem_names(self):
        names = ["se_" + e for e in self.ENGINES]
        names += ["sd_" + n for n in self.dma_seq]
        return names

    def emit_engine(self, ename, eng, sems):
        own_sem = sems["se_" + ename]
        waited = {}
        inc_here = set(self.inc_seqs[ename])
        for i, rec in enumerate(self.recs[ename]):
            waits = {}
            for tok in rec["deps"]:
                if tok[0] == "eng" and tok[1] == ename and ename == "pe":
                    continue
                sname, val = self.wait_value(tok)
                waits[sname] = max(waits.get(sname, 0), val)
            for sname, val in sorted(waits.items()):
                if val > waited.get(sname, 0):
                    eng.wait_ge(sems[sname], val)
                    waited[sname] = val
            if rec["emit"] is None:
                continue
            inst = rec["emit"](eng)
            tok = rec["tok"]
            if tok[0] == "dma":
                inst.then_inc(sems["sd_" + tok[1]], 16)
            elif tok[2] in inc_here:
                inst.then_inc(own_sem, 1)


class Ctx:
    pass


def alloc_common(nc, ctx, stack):
    sb = lambda name, shape, dt: stack.enter_context(
        nc.sbuf_tensor(name, shape, dt))
    ctx.X = sb("X", [128, 8 * TOK], F32)
    ctx.id_f32 = sb("id_f32", [128, 128], F32)
    ctx.id_bf = sb("id_bf", [128, 128], BF16)
    ctx.ones_row_f = sb("ones_row_f", [1, 128], F32)
    ctx.ones_row_b = sb("ones_row_b", [1, 128], BF16)
    ctx.nones_row_f = sb("nones_row_f", [1, 128], F32)
    ctx.ones_col_b = sb("ones_col_b", [128, 1], BF16)
    ctx.mask_s = sb("mask_s", [128, 128], BF16)
    ctx.mask_sT = sb("mask_sT", [128, 128], BF16)
    ctx.mask_iT = sb("mask_iT", [128, 128], BF16)
    rows = sb("rows", [1, 1024], F32)
    ctx.rstd_row = rows[0:1, 0:512]
    ctx.t_row = rows[0:1, 512:1024]
    ctx.xbf = sb("xbf", [128, 512], BF16)
    ctx.xsq = sb("xsq", [128, 512], BF16)
    ctx.tmpf = sb("tmpf", [128, 512], F32)
    ctx.eps5 = sb("eps5", [128, 1], F32)
    ctx.eps12 = sb("eps12", [128, 1], F32)


def psum_alloc(nc, ctx, stack):
    ctx.PS = [stack.enter_context(nc.psum_tensor(f"ps{i}", [128, 512], F32))
              for i in range(7)]
    ctx.PSB = stack.enter_context(nc.psum_tensor("psb", [128, 1024], BF16))
    ctx.m_row = ctx.PS[4][0:1, 0:512]
    ctx.var_row = ctx.PS[5][0:1, 0:512]
    ctx.psum_tensors = ctx.PS + [ctx.PSB]


def dram_io_l1(nc, ctx):
    d = lambda n, s, dt, k: nc.dram_tensor(n, s, dt, kind=k).ap()
    ctx.x_in = d("x_in", [TOK, D], F32, "ExternalInput")
    ctx.w_upT = d("w_upT", [NE, D, E], F32, "ExternalInput")
    ctx.w_dnT = d("w_dnT", [NE, E, D], F32, "ExternalInput")
    ctx.w_router = d("w_router", [D, 16], F32, "ExternalInput")
    ctx.w_q = d("w_q", [D, 512], F32, "ExternalInput")
    ctx.w_k = d("w_k", [D, 512], F32, "ExternalInput")
    ctx.w_v = d("w_v", [D, 1024], F32, "ExternalInput")
    ctx.w_b = d("w_b", [D, 4], F32, "ExternalInput")
    ctx.w_g = d("w_g", [D, 1024], F32, "ExternalInput")
    ctx.c_idf = d("c_idf", [128, 128], F32, "ExternalInput")
    ctx.c_idb = d("c_idb", [128, 128], BF16, "ExternalInput")
    ctx.c_mask_s = d("c_mask_s", [128, 128], BF16, "ExternalInput")
    ctx.c_mask_sT = d("c_mask_sT", [128, 128], BF16, "ExternalInput")
    ctx.c_mask_iT = d("c_mask_iT", [128, 128], BF16, "ExternalInput")
    ctx.c_ones_rf = d("c_ones_rf", [1, 128], F32, "ExternalInput")
    ctx.c_ones_rb = d("c_ones_rb", [1, 128], BF16, "ExternalInput")
    ctx.c_nones_rf = d("c_nones_rf", [1, 128], F32, "ExternalInput")
    ctx.c_ones_cb = d("c_ones_cb", [128, 1], BF16, "ExternalInput")
    ctx.x1_out = d("x1_out", [128, 8 * TOK], F32, "ExternalOutput")
    ctx.gsilu_out = d("gsilu_out", [NCHUNK, 128, 1024], BF16, "ExternalOutput")
    ctx.cstore_out = d("cstore_out", [NCHUNK, 128, 1536], BF16, "ExternalOutput")
    ctx.paT_out = d("paT_out", [128, 512], BF16, "ExternalOutput")
    ctx.ta_out = d("ta_out", [128, 1024], F32, "ExternalOutput")
    ctx.xn_scr = nc.dram_tensor("xn_scr", [G512, 128, 4096], BF16).ap()


def load_constants(ctx, s):
    g = "pool"
    pairs = [
        (ctx.id_f32, ctx.c_idf), (ctx.id_bf, ctx.c_idb),
        (ctx.ones_row_f, ctx.c_ones_rf), (ctx.ones_row_b, ctx.c_ones_rb),
        (ctx.nones_row_f, ctx.c_nones_rf), (ctx.ones_col_b, ctx.c_ones_cb),
        (ctx.mask_s, ctx.c_mask_s), (ctx.mask_sT, ctx.c_mask_sT),
        (ctx.mask_iT, ctx.c_mask_iT),
    ]
    for dst, src in pairs:
        s.op(g, (lambda d, sr: lambda eng: eng.dma_start(out=d[:, :], in_=sr))
             (dst, src), writes=[dst[:, :]], dma_sem="cst", tag="const")
    s.op("dve", lambda eng: eng.memset(ctx.eps5[:, :], 1e-5),
         writes=[ctx.eps5[:, :]], tag="eps5")
    s.op("dve", lambda eng: eng.memset(ctx.eps12[:, :], 1e-12),
         writes=[ctx.eps12[:, :]], tag="eps12")


def emit_load_x(ctx, s):
    nc = ctx.nc
    xt = ctx.xtok
    for j in range(NCHUNK):
        buf = xt[j % 2]
        s.op("sp", (lambda b, jj: lambda eng: eng.dma_start(
            out=b[:, :], in_=ctx.x_in[jj * 128:(jj + 1) * 128, :]))(buf, j),
            writes=[buf[:, :]], dma_sem="xin", tag=f"xin{j}")
        for k in range(8):
            ps = ctx.PS[k % 2]
            s.op("pe", (lambda b, kk, p: lambda eng: eng.transpose(
                p[:, 0:128], b[:, kk * 128:(kk + 1) * 128], ctx.id_f32[:, :]))
                (buf, k, ps),
                reads=[buf[:, k * 128:(k + 1) * 128], ctx.id_f32[:, :]],
                writes=[ps[:, 0:128]], tag=f"xT{j}.{k}")
            dst = ctx.X[:, k * TOK + j * 128: k * TOK + (j + 1) * 128]
            s.op("dve", (lambda p, dd: lambda eng: eng.tensor_copy(dd, p[:, 0:128]))
                 (ps, dst),
                 reads=[ps[:, 0:128]], writes=[dst], tag=f"xTe{j}.{k}")


def emit_ln(ctx, s, gi, dst_hbm=None, dst_sb=None):
    nc = ctx.nc
    X = ctx.X
    PS_sum, PS_sq, PS_s, PS_t = ctx.PS[0], ctx.PS[1], ctx.PS[2], ctx.PS[3]
    for k in range(8):
        xv = X[:, k * TOK + gi * 512: k * TOK + (gi + 1) * 512]
        s.op("dve", (lambda xv_: lambda eng: eng.tensor_copy(
            ctx.xbf[:, :], xv_))(xv),
            reads=[xv], writes=[ctx.xbf[:, :]], tag=f"lncast{gi}.{k}")
        s.op("act", (lambda xv_: lambda eng: eng.square(ctx.xsq[:, :], xv_))(xv),
             reads=[xv], writes=[ctx.xsq[:, :]], tag=f"lnsq{gi}.{k}")
        s.op("pe", (lambda kk: lambda eng: eng.matmul(
            PS_sum[0:1, :], ctx.ones_col_b[:, :], ctx.xbf[:, :],
            start=(kk == 0), stop=(kk == 7)))(k),
            reads=[ctx.ones_col_b[:, :], ctx.xbf[:, :]],
            writes=[PS_sum[0:1, :]], tag=f"lnsum{gi}.{k}")
        s.op("pe", (lambda kk: lambda eng: eng.matmul(
            PS_sq[0:1, :], ctx.ones_col_b[:, :], ctx.xsq[:, :],
            start=(kk == 0), stop=(kk == 7)))(k),
            reads=[ctx.ones_col_b[:, :], ctx.xsq[:, :]],
            writes=[PS_sq[0:1, :]], tag=f"lnsumsq{gi}.{k}")
    s.op("act", lambda eng: eng.mul(ctx.m_row[:, :], PS_sum[0:1, :], -1.0 / D),
         reads=[PS_sum[0:1, :]], writes=[ctx.m_row[:, :]], tag=f"lnm{gi}")
    s.op("dve", lambda eng: eng.tensor_scalar(
        ctx.var_row[:, :], PS_sq[0:1, :], 1.0 / D, None, OP.mult),
        reads=[PS_sq[0:1, :]], writes=[ctx.var_row[:, :]], tag=f"lnv{gi}")
    s.op("act", lambda eng: eng.square(ctx.t_row[:, :], ctx.m_row[:, :]),
         reads=[ctx.m_row[:, :]], writes=[ctx.t_row[:, :]], tag=f"lnm2{gi}")
    s.op("dve", lambda eng: eng.tensor_sub(
        ctx.var_row[:, :], ctx.var_row[:, :], ctx.t_row[:, :]),
        reads=[ctx.var_row[:, :], ctx.t_row[:, :]],
        writes=[ctx.var_row[:, :]], tag=f"lnv2{gi}")
    s.op("act", lambda eng: eng.activation(
        ctx.rstd_row[:, :], ctx.var_row[:, :], AF.Sqrt,
        bias=ctx.eps5[0:1, 0:1]),
        reads=[ctx.var_row[:, :], ctx.eps5[0:1, 0:1]],
        writes=[ctx.rstd_row[:, :]], tag=f"lnsr{gi}")
    s.op("dve", lambda eng: eng.reciprocal(ctx.rstd_row[:, :], ctx.rstd_row[:, :]),
         reads=[ctx.rstd_row[:, :]], writes=[ctx.rstd_row[:, :]], tag=f"lnrs{gi}")
    s.op("dve", lambda eng: eng.tensor_mul(
        ctx.t_row[:, :], ctx.m_row[:, :], ctx.rstd_row[:, :]),
        reads=[ctx.m_row[:, :], ctx.rstd_row[:, :]],
        writes=[ctx.t_row[:, :]], tag=f"lnt{gi}")
    s.op("pe", lambda eng: eng.matmul(
        PS_s[:, :], ctx.ones_row_f[:, :], ctx.rstd_row[:, :],
        start=True, stop=True),
        reads=[ctx.ones_row_f[:, :], ctx.rstd_row[:, :]],
        writes=[PS_s[:, :]], tag=f"lnbs{gi}")
    s.op("pe", lambda eng: eng.matmul(
        PS_t[:, :], ctx.ones_row_f[:, :], ctx.t_row[:, :],
        start=True, stop=True),
        reads=[ctx.ones_row_f[:, :], ctx.t_row[:, :]],
        writes=[PS_t[:, :]], tag=f"lnbt{gi}")
    for k in range(8):
        xv = X[:, k * TOK + gi * 512: k * TOK + (gi + 1) * 512]
        dst = ctx.xn_w[:, k * 512:(k + 1) * 512]
        s.op("dve", (lambda xv_: lambda eng: eng.tensor_mul(
            ctx.tmpf[:, :], xv_, PS_s[:, :]))(xv),
            reads=[xv, PS_s[:, :]], writes=[ctx.tmpf[:, :]], tag=f"lnap{gi}.{k}")
        s.op("dve", (lambda dd: lambda eng: eng.tensor_add(
            dd, ctx.tmpf[:, :], PS_t[:, :]))(dst),
            reads=[ctx.tmpf[:, :], PS_t[:, :]], writes=[dst], tag=f"lnab{gi}.{k}")
    if dst_hbm is not None:
        s.op("sp", (lambda gg, xw: lambda eng: eng.dma_start(
            out=dst_hbm[gg, :, :], in_=xw))(gi, ctx.xn_w[:, :]),
            reads=[ctx.xn_w[:, :]], writes=[dst_hbm[gi]],
            dma_sem="xnspill", tag=f"lnsp{gi}", force_inc=True)


def emit_router_gates(ctx, s, gi, tag=""):
    PS6 = ctx.PS[6]
    for j in range(4):
        for k in range(8):
            st = ctx.xn_w[:, k * 512 + j * 128: k * 512 + (j + 1) * 128]
            mv = ctx.router_bf[:, k * 16:(k + 1) * 16]
            s.op("pe", (lambda st_, mv_, kk: lambda eng: eng.matmul(
                PS6[:, 0:16], st_, mv_, start=(kk == 0), stop=(kk == 7)))
                (st, mv, k),
                reads=[st, mv], writes=[PS6[:, 0:16]], tag=f"rl{tag}{gi}.{j}")
        s.op("dve", lambda eng: eng.reduce_max(
            ctx.negmax[:, :], PS6[:, 0:16], axis=AX.X, negate=True),
            reads=[PS6[:, 0:16]], writes=[ctx.negmax[:, :]], tag="rmax")
        s.op("act", lambda eng: eng.activation(
            ctx.probs[:, :], PS6[:, 0:16], AF.Exp, bias=ctx.negmax[:, 0:1],
            accum_out=ctx.ssum[:, 0:1]),
            reads=[PS6[:, 0:16], ctx.negmax[:, 0:1]],
            writes=[ctx.probs[:, :], ctx.ssum[:, 0:1]], tag="rexp")
        s.op("dve", lambda eng: eng.reciprocal(ctx.rsum[:, :], ctx.ssum[:, :]),
             reads=[ctx.ssum[:, :]], writes=[ctx.rsum[:, :]], tag="rrec")
        s.op("dve", lambda eng: eng.tensor_scalar(
            ctx.ph[:, :], ctx.probs[:, :], ctx.rsum[:, 0:1], 0.5,
            OP.mult, OP.mult),
            reads=[ctx.probs[:, :], ctx.rsum[:, 0:1]],
            writes=[ctx.ph[:, :]], tag="rph")
        s.op("dve", lambda eng: eng.max(out=ctx.m8[:, :], in_=ctx.ph[:, :]),
             reads=[ctx.ph[:, :]], writes=[ctx.m8[:, :]], tag="rm8")
        s.op("dve", lambda eng: eng.memset(ctx.m8[:, 4:8], -1e30),
             writes=[ctx.m8[:, 4:8]], tag="rm8z")
        s.op("dve", lambda eng: eng.match_replace(
            out=ctx.pz[:, :], in_to_replace=ctx.m8[:, 0:8],
            in_values=ctx.ph[:, :], imm_value=0.0),
            reads=[ctx.m8[:, 0:8], ctx.ph[:, :]],
            writes=[ctx.pz[:, :]], tag="rmr")
        s.op("dve", lambda eng: eng.tensor_sub(
            ctx.gt[:, :], ctx.ph[:, :], ctx.pz[:, :]),
            reads=[ctx.ph[:, :], ctx.pz[:, :]], writes=[ctx.gt[:, :]],
            tag="rg")
        s.op("pe", lambda eng: eng.transpose(
            PS6[0:16, 128:256], ctx.gt[:, :], ctx.id_f32[:, :]),
            reads=[ctx.gt[:, :], ctx.id_f32[:, :]],
            writes=[PS6[0:16, 128:256]], tag="rgt")
        dst = ctx.gate_em[0:16, gi * 512 + j * 128: gi * 512 + (j + 1) * 128]
        s.op("dve", (lambda dd: lambda eng: eng.tensor_copy(dd, PS6[0:16, 128:256]))(dst),
             reads=[PS6[0:16, 128:256]], writes=[dst], tag="rge")


def emit_ffn_pass(ctx, s, xn_hbm, tag=""):
    PS = ctx.PS
    PSB = ctx.PSB
    for blk in range(NBLK):
        wb = blk % 2
        up_buf, dn_buf = ctx.up_bufs[wb], ctx.dn_bufs[wb]
        for e3 in range(BLKE):
            e = blk * BLKE + e3
            src = ctx.w_upT[e].rearrange("(k p) e2 -> p k e2", p=128)
            dst = up_buf[:, e3 * 4096:(e3 + 1) * 4096]
            s.op("pool", (lambda d_, s_: lambda eng: eng.dma_start(out=d_, in_=s_))
                 (dst.rearrange("p (k e2) -> p k e2", k=8), src),
                 writes=[dst], dma_sem="wup", tag=f"wup{tag}{e}")
            src = ctx.w_dnT[e].rearrange("(k p) h -> p k h", p=128)
            dst = dn_buf[:, e3 * 4096:(e3 + 1) * 4096]
            s.op("pool", (lambda d_, s_: lambda eng: eng.dma_start(out=d_, in_=s_))
                 (dst.rearrange("p (k h) -> p k h", k=4), src),
                 writes=[dst], dma_sem="wdn", tag=f"wdn{tag}{e}")
        for gi in range(G512):
            xb = ctx.xn_r[gi % 2]
            s.op("sp", (lambda b_, g_: lambda eng: eng.dma_start(
                out=b_[:, :], in_=xn_hbm[g_, :, :]))(xb, gi),
                reads=[xn_hbm[gi]], writes=[xb[:, :]],
                dma_sem="xnrd", tag=f"xnr{tag}{blk}.{gi}")
            lo = max(blk * BLKE, 2) - 2
            hi = (blk + 1) * BLKE - 2
            nrow = hi - lo
            if nrow > 0:
                src = ctx.gate_em[lo:hi, gi * 512:(gi + 1) * 512]
                dst = ctx.gflat[0:1, 0:nrow * 512]
                s.op("pool", (lambda d_, s_, r_: lambda eng: eng.dma_start(
                    out=d_.rearrange("o (r t) -> o r t", r=r_), in_=s_))
                    (dst, src, nrow),
                    reads=[src], writes=[dst], dma_sem="gfl",
                    tag=f"gfl{tag}{blk}.{gi}")
            for e3 in range(BLKE):
                e = blk * BLKE + e3
                actg = ctx.act_g[e3]
                for m in range(4):
                    stat = [up_buf[:, e3 * 4096 + k * 512 + m * 128:
                                   e3 * 4096 + k * 512 + (m + 1) * 128]
                            for k in range(8)]
                    mov = [xb[:, k * 512:(k + 1) * 512] for k in range(8)]

                    def mk_up(stat_, mov_, ps_):
                        def f(eng):
                            for kk in range(8):
                                inst = eng.matmul(ps_[:, :], stat_[kk], mov_[kk],
                                                  start=(kk == 0), stop=(kk == 7))
                            return inst
                        return f
                    s.op("pe", mk_up(stat, mov, PS[m]),
                         reads=stat + mov, writes=[PS[m][:, :]],
                         tag=f"up{tag}{blk}.{gi}.{e3}.{m}")
                    s.op("act", (lambda m_, a_: lambda eng: eng.activation(
                        a_[:, m_ * 512:(m_ + 1) * 512], PS[m_][:, :], AF.Gelu))
                        (m, actg),
                        reads=[PS[m][:, :]],
                        writes=[actg[:, m * 512:(m + 1) * 512]],
                        tag=f"gel{tag}{blk}.{gi}.{e3}.{m}")
                if e >= 2:
                    ri = e - 2 - lo
                    rhs = ctx.gflat[0:1, ri * 512:(ri + 1) * 512]
                    s.op("pe", (lambda r_: lambda eng: eng.matmul(
                        PS[6][:, :], ctx.ones_row_b[:, :], r_,
                        start=True, stop=True))(rhs),
                        reads=[ctx.ones_row_b[:, :], rhs],
                        writes=[PS[6][:, :]], tag=f"gbc{tag}{blk}.{gi}.{e3}")
                    s.op("act", lambda eng: eng.copy(
                        ctx.gbc_sb[:, :], PS[6][:, :]),
                        reads=[PS[6][:, :]], writes=[ctx.gbc_sb[:, :]],
                        tag=f"gbe{tag}{blk}.{gi}.{e3}")
                    for m in range(4):
                        s.op("dve", (lambda m_, a_: lambda eng: eng.tensor_mul(
                            a_[:, m_ * 512:(m_ + 1) * 512],
                            a_[:, m_ * 512:(m_ + 1) * 512],
                            ctx.gbc_sb[:, :]))(m, actg),
                            reads=[actg[:, m * 512:(m + 1) * 512],
                                   ctx.gbc_sb[:, :]],
                            writes=[actg[:, m * 512:(m + 1) * 512]],
                            tag=f"gm{tag}{blk}.{gi}.{e3}.{m}")
                else:
                    s.op("dve", (lambda a_: lambda eng: eng.tensor_scalar(
                        a_[:, :], a_[:, :], 0.5, None, OP.mult))(actg),
                        reads=[actg[:, :]], writes=[actg[:, :]],
                        tag=f"gs{tag}{blk}.{gi}.{e3}")
            for h in range(8):
                ps = PS[4 + (h % 2)]
                stats = []
                movs = []
                for e3 in range(BLKE):
                    for kE in range(4):
                        stats.append(ctx.dn_bufs[wb][:, e3 * 4096 + kE * 1024 +
                                                     h * 128: e3 * 4096 +
                                                     kE * 1024 + (h + 1) * 128])
                        movs.append(ctx.act_g[e3][:, kE * 512:(kE + 1) * 512])

                def mk_dn(stats_, movs_, ps_):
                    def f(eng):
                        n = len(stats_)
                        for i in range(n):
                            inst = eng.matmul(ps_[:, :], stats_[i], movs_[i],
                                              start=(i == 0), stop=(i == n - 1))
                        return inst
                    return f
                s.op("pe", mk_dn(stats, movs, ps),
                     reads=stats + movs, writes=[ps[:, :]],
                     tag=f"dn{tag}{blk}.{gi}.{h}")
                xv = ctx.X[:, h * TOK + gi * 512: h * TOK + (gi + 1) * 512]
                s.op("dve", (lambda xv_, ps_: lambda eng: eng.tensor_add(
                    xv_, xv_, ps_[:, :]))(xv, ps),
                    reads=[xv, ps[:, :]], writes=[xv],
                    tag=f"dna{tag}{blk}.{gi}.{h}")


def alloc_ffn(nc, ctx, stack):
    sb = lambda name, shape, dt: stack.enter_context(
        nc.sbuf_tensor(name, shape, dt))
    ctx.gflat = sb("gflat", [1, 1536], BF16)
    ctx.gbc_sb = sb("gbc_sb", [128, 512], BF16)
    ctx.up_bufs = [sb(f"up_buf{i}", [128, 3 * 4096], BF16) for i in range(2)]
    ctx.dn_bufs = [sb(f"dn_buf{i}", [128, 3 * 4096], BF16) for i in range(2)]
    ctx.xn_r = [sb(f"xn_r{i}", [128, 4096], BF16) for i in range(2)]
    ctx.act_g = [sb(f"act_g{i}", [128, 2048], BF16) for i in range(3)]


def alloc_attn(nc, ctx, stack):
    sb = lambda name, shape, dt: stack.enter_context(
        nc.sbuf_tensor(name, shape, dt))
    ctx.wq_bf = sb("wq_bf", [128, 8 * 512], BF16)
    ctx.wk_bf = sb("wk_bf", [128, 8 * 512], BF16)
    ctx.wv_bf = sb("wv_bf", [128, 8 * 1024], BF16)
    ctx.wg_bf = sb("wg_bf", [128, 8 * 1024], BF16)
    ctx.wb_bf = sb("wb_bf", [128, 8 * 4], BF16)
    ctx.Qs = sb("Qs", [128, 512], BF16)
    ctx.Kfm = sb("Kfm", [128, 512], BF16)
    ctx.Ktok = sb("Ktok", [128, 512], BF16)
    ctx.Kraw = sb("Kraw", [128, 512], F32)
    ctx.Rb = sb("Rb", [128, 1536], BF16)
    ctx.A_b = [sb(f"A_b{i}", [128, 512], BF16) for i in range(2)]
    ctx.AT_b = [sb(f"AT_b{i}", [128, 512], BF16) for i in range(2)]
    ctx.Eb = [sb(f"Eb{i}", [128, 512], BF16) for i in range(2)]
    ctx.Ltmp = sb("Ltmp", [128, 512], BF16)
    ctx.LTtmp = sb("LTtmp", [128, 512], BF16)
    ctx.AqkT = sb("AqkT", [128, 512], BF16)
    ctx.nApT = sb("nApT", [128, 512], BF16)
    ctx.GT = sb("GT", [128, 512], BF16)
    ctx.U0W = sb("U0W", [128, 1536], BF16)
    ctx.O0p = sb("O0p", [128, 1024], BF16)
    ctx.GPTb = sb("GPTb", [128, 512], BF16)
    ctx.Tacc = sb("Tacc", [128, 1024], F32)
    ctx.Tacc_sh = sb("Tacc_sh", [128, 1024], BF16)
    ctx.Pa = sb("Pa", [128, 512], F32)
    ctx.Pa_sh = sb("Pa_sh", [128, 512], BF16)
    ctx.btok = sb("btok", [128, 4], F32)
    ctx.kcols = sb("kcols", [128, 12], F32)
    krows = sb("krows", [1, 1536], F32)
    ctx.krow = krows[0:1, 0:512]
    ctx.krow2 = krows[0:1, 512:1024]
    ctx.brow = krows[0:1, 1024:1536]
    ctx.gs_sb = sb("gs_sb", [128, 1024], BF16)
    ctx.paT_sb = sb("paT_sb", [128, 512], BF16)


def emit_attn_wload(ctx, s):
    loads = [
        (ctx.wq_bf, ctx.w_q, 512, "wq"), (ctx.wk_bf, ctx.w_k, 512, "wk"),
        (ctx.wv_bf, ctx.w_v, 1024, "wv"), (ctx.wg_bf, ctx.w_g, 1024, "wg"),
        (ctx.wb_bf, ctx.w_b, 4, "wb"),
    ]
    for dst, src, n, tg in loads:
        s.op("pool", (lambda d_, s_, n_: lambda eng: eng.dma_start(
            out=d_[:, :].rearrange("p (k n) -> p k n", k=8),
            in_=s_.rearrange("(k p) n -> p k n", p=128)))(dst, src, n),
            writes=[dst[:, :]], dma_sem="watt", tag=tg)


def emit_phase_a_init(ctx, s):
    s.op("dve", lambda eng: eng.memset(ctx.Tacc[:, :], 0.0),
         writes=[ctx.Tacc[:, :]], tag="taz")
    s.op("dve", lambda eng: eng.memset(ctx.Tacc_sh[:, :], 0.0),
         writes=[ctx.Tacc_sh[:, :]], tag="tasz")
    s.op("dve", lambda eng: eng.memset(ctx.Pa[:, :], 0.0),
         writes=[ctx.Pa[:, :]], tag="paz0")
    s.op("dve", lambda eng: eng.memset(ctx.Pa_sh[:, :], 0.0),
         writes=[ctx.Pa_sh[:, :]], tag="pasz0")
    for h in range(H):
        dsts = [ctx.Pa[:, h * 128:(h + 1) * 128],
                ctx.Pa_sh[:, h * 128:(h + 1) * 128]]
        s.op("dve", (lambda d_: lambda eng: eng.tensor_copy(d_, ctx.id_f32[:, :]))
             (dsts[0]), reads=[ctx.id_f32[:, :]], writes=[dsts[0]],
             tag=f"pai{h}")
        s.op("dve", (lambda d_: lambda eng: eng.tensor_copy(d_, ctx.id_bf[:, :]))
             (dsts[1]), reads=[ctx.id_bf[:, :]], writes=[dsts[1]],
             tag=f"pasi{h}")


def emit_chunk(ctx, s, j, upto="full"):
    PS = ctx.PS
    jj = j % 4
    xa = [ctx.xn_w[:, k * 512 + jj * 128: k * 512 + (jj + 1) * 128]
          for k in range(8)]

    def mmgrp(eng_dst, stats, movs, tg, bounds=None):
        def f(eng):
            n = len(stats)
            for i in range(n):
                inst = eng.matmul(eng_dst[i] if isinstance(eng_dst, list)
                                  else eng_dst,
                                  stats[i], movs[i],
                                  start=(i == 0 if bounds is None else bounds[i][0]),
                                  stop=(i == n - 1 if bounds is None else bounds[i][1]))
            return inst
        return f

    stats, movs, dsts = [], [], []
    for m in range(4):
        for k in range(8):
            stats.append(ctx.wq_bf[:, k * 512 + m * 128: k * 512 + (m + 1) * 128])
            movs.append(xa[k])
            dsts.append(PS[0][:, m * 128:(m + 1) * 128])
    bounds = [(k == 0, k == 7) for m in range(4) for k in range(8)]
    s.op("pe", mmgrp(dsts, stats, movs, "q", bounds),
         reads=stats + movs, writes=[PS[0][:, :]], tag=f"qp{j}")
    s.op("act", lambda eng: eng.mul(ctx.Qs[:, :], PS[0][:, :], SCALE),
         reads=[PS[0][:, :]], writes=[ctx.Qs[:, :]], tag=f"qe{j}")

    stats, movs, dsts = [], [], []
    for m in range(4):
        for k in range(8):
            stats.append(ctx.wk_bf[:, k * 512 + m * 128: k * 512 + (m + 1) * 128])
            movs.append(xa[k])
            dsts.append(PS[1][:, m * 128:(m + 1) * 128])
    s.op("pe", mmgrp(dsts, stats, movs, "kf", bounds),
         reads=stats + movs, writes=[PS[1][:, :]], tag=f"kfp{j}")
    s.op("act", lambda eng: eng.copy(ctx.Kraw[:, :], PS[1][:, :]),
         reads=[PS[1][:, :]], writes=[ctx.Kraw[:, :]], tag=f"kfe{j}")
    s.op("act", lambda eng: eng.square(ctx.xbf[:, :], ctx.Kraw[:, :]),
         reads=[ctx.Kraw[:, :]], writes=[ctx.xbf[:, :]], tag=f"ksq{j}")
    for h in range(H):
        s.op("pe", (lambda h_: lambda eng: eng.matmul(
            PS[2][0:1, h_ * 128:(h_ + 1) * 128], ctx.ones_col_b[:, :],
            ctx.xbf[:, h_ * 128:(h_ + 1) * 128], start=True, stop=True))(h),
            reads=[ctx.ones_col_b[:, :], ctx.xbf[:, h * 128:(h + 1) * 128]],
            writes=[PS[2][0:1, h * 128:(h + 1) * 128]], tag=f"kss{j}.{h}")
    s.op("act", lambda eng: eng.activation(
        ctx.krow, PS[2][0:1, :], AF.Sqrt, bias=ctx.eps12[0:1, 0:1]),
        reads=[PS[2][0:1, :], ctx.eps12[0:1, 0:1]], writes=[ctx.krow],
        tag=f"ksr{j}")
    s.op("dve", lambda eng: eng.reciprocal(ctx.krow2, ctx.krow),
         reads=[ctx.krow], writes=[ctx.krow2], tag=f"krc{j}")
    s.op("pe", lambda eng: eng.matmul(
        PS[3][:, :], ctx.ones_row_f[:, :], ctx.krow2, start=True, stop=True),
        reads=[ctx.ones_row_f[:, :], ctx.krow2], writes=[PS[3][:, :]],
        tag=f"kbc{j}")
    s.op("dve", lambda eng: eng.tensor_mul(
        ctx.Kfm[:, :], ctx.Kraw[:, :], PS[3][:, :]),
        reads=[ctx.Kraw[:, :], PS[3][:, :]], writes=[ctx.Kfm[:, :]],
        tag=f"knm{j}")

    stats = [xa[k] for k in range(8)]
    movs = [ctx.wk_bf[:, k * 512:(k + 1) * 512] for k in range(8)]
    s.op("pe", mmgrp(PS[1][:, :], stats, movs, "kt"),
         reads=stats + movs, writes=[PS[1][:, :]], tag=f"ktp{j}")
    s.op("act", lambda eng: eng.square(ctx.xsq[:, :], PS[1][:, :]),
         reads=[PS[1][:, :]], writes=[ctx.xsq[:, :]], tag=f"ktsq{j}")
    for h in range(H):
        s.op("dve", (lambda h_: lambda eng: eng.reduce_sum(
            ctx.kcols[:, h_:h_ + 1], ctx.xsq[:, h_ * 128:(h_ + 1) * 128],
            axis=AX.X))(h),
            reads=[ctx.xsq[:, h * 128:(h + 1) * 128]],
            writes=[ctx.kcols[:, h:h + 1]], tag=f"ktr{j}.{h}")
    s.op("act", lambda eng: eng.activation(
        ctx.kcols[:, 4:8], ctx.kcols[:, 0:4], AF.Sqrt,
        bias=ctx.eps12[:, 0:1]),
        reads=[ctx.kcols[:, 0:4], ctx.eps12[:, 0:1]],
        writes=[ctx.kcols[:, 4:8]], tag=f"ktsr{j}")
    s.op("dve", lambda eng: eng.reciprocal(ctx.kcols[:, 8:12], ctx.kcols[:, 4:8]),
         reads=[ctx.kcols[:, 4:8]], writes=[ctx.kcols[:, 8:12]], tag=f"ktrc{j}")
    for h in range(H):
        s.op("dve", (lambda h_: lambda eng: eng.tensor_scalar(
            ctx.Ktok[:, h_ * 128:(h_ + 1) * 128],
            PS[1][:, h_ * 128:(h_ + 1) * 128],
            ctx.kcols[:, 8 + h_:9 + h_], None, OP.mult))(h),
            reads=[PS[1][:, h * 128:(h + 1) * 128], ctx.kcols[:, 8 + h:9 + h]],
            writes=[ctx.Ktok[:, h * 128:(h + 1) * 128]], tag=f"ktn{j}.{h}")

    stats = [xa[k] for k in range(8)]
    movs = [ctx.wb_bf[:, k * 4:(k + 1) * 4] for k in range(8)]
    s.op("pe", mmgrp(PS[6][:, 0:4], stats, movs, "bt"),
         reads=stats + movs, writes=[PS[6][:, 0:4]], tag=f"btp{j}")
    s.op("act", lambda eng: eng.activation(
        ctx.btok[:, :], PS[6][:, 0:4], AF.Sigmoid),
        reads=[PS[6][:, 0:4]], writes=[ctx.btok[:, :]], tag=f"bte{j}")
    stats, movs, dsts = [], [], []
    bounds = []
    for h in range(H):
        for k in range(8):
            stats.append(ctx.wb_bf[:, k * 4 + h: k * 4 + h + 1])
            movs.append(xa[k])
            dsts.append(PS[0][0:1, h * 128:(h + 1) * 128])
            bounds.append((k == 0, k == 7))
    s.op("pe", mmgrp(dsts, stats, movs, "br", bounds),
         reads=stats + movs, writes=[PS[0][0:1, :]], tag=f"brp{j}",
         extra_deps=(), )
    s.op("act", lambda eng: eng.activation(
        ctx.brow, PS[0][0:1, :], AF.Sigmoid),
        reads=[PS[0][0:1, :]], writes=[ctx.brow], tag=f"bre{j}")
    s.op("pe", lambda eng: eng.matmul(
        PS[1][:, :], ctx.nones_row_f[:, :], ctx.brow, start=True, stop=True),
        reads=[ctx.nones_row_f[:, :], ctx.brow], writes=[PS[1][:, :]],
        tag=f"bbc{j}")

    stats = [xa[k] for k in range(8)]
    movs = [ctx.wv_bf[:, k * 1024:(k + 1) * 1024][:, 0:512] for k in range(8)]
    s.op("pe", mmgrp(PS[2][:, :], stats, movs, "v0"),
         reads=stats + movs, writes=[PS[2][:, :]], tag=f"vp0{j}")
    movs = [ctx.wv_bf[:, k * 1024:(k + 1) * 1024][:, 512:1024] for k in range(8)]
    s.op("pe", mmgrp(PS[3][:, :], stats, movs, "v1"),
         reads=stats + movs, writes=[PS[3][:, :]], tag=f"vp1{j}")
    for h in range(H):
        ps = PS[2] if h < 2 else PS[3]
        off = (h % 2) * 256
        s.op("dve", (lambda h_, p_, o_: lambda eng: eng.tensor_scalar(
            ctx.Rb[:, h_ * 384: h_ * 384 + 256], p_[:, o_:o_ + 256],
            ctx.btok[:, h_:h_ + 1], None, OP.mult))(h, ps, off),
            reads=[ps[:, off:off + 256], ctx.btok[:, h:h + 1]],
            writes=[ctx.Rb[:, h * 384: h * 384 + 256]], tag=f"rv{j}.{h}")
        s.op("dve", (lambda h_: lambda eng: eng.tensor_scalar(
            ctx.Rb[:, h_ * 384 + 256:(h_ + 1) * 384],
            ctx.Ktok[:, h_ * 128:(h_ + 1) * 128],
            ctx.btok[:, h_:h_ + 1], None, OP.mult))(h),
            reads=[ctx.Ktok[:, h * 128:(h + 1) * 128], ctx.btok[:, h:h + 1]],
            writes=[ctx.Rb[:, h * 384 + 256:(h + 1) * 384]], tag=f"rk{j}.{h}")

    movs = [ctx.wg_bf[:, k * 1024:(k + 1) * 1024][:, 0:512] for k in range(8)]
    s.op("pe", mmgrp(PS[4][:, :], stats, movs, "g0"),
         reads=stats + movs, writes=[PS[4][:, :]], tag=f"gp0{j}")
    movs = [ctx.wg_bf[:, k * 1024:(k + 1) * 1024][:, 512:1024] for k in range(8)]
    s.op("pe", mmgrp(PS[5][:, :], stats, movs, "g1"),
         reads=stats + movs, writes=[PS[5][:, :]], tag=f"gp1{j}")
    s.op("act", lambda eng: eng.activation(
        ctx.gs_sb[:, 0:512], PS[4][:, :], AF.Silu),
        reads=[PS[4][:, :]], writes=[ctx.gs_sb[:, 0:512]], tag=f"gs0{j}")
    s.op("act", lambda eng: eng.activation(
        ctx.gs_sb[:, 512:1024], PS[5][:, :], AF.Silu),
        reads=[PS[5][:, :]], writes=[ctx.gs_sb[:, 512:1024]], tag=f"gs1{j}")
    s.op("sp", (lambda j_: lambda eng: eng.dma_start(
        out=ctx.gsilu_out[j_, :, :], in_=ctx.gs_sb[:, :]))(j),
        reads=[ctx.gs_sb[:, :]], dma_sem="gso", force_inc=True, tag=f"gso{j}")
    if upto == "proj":
        return

    for h in range(H):
        s.op("pe", (lambda h_: lambda eng: eng.matmul(
            PS[4][:, h_ * 128:(h_ + 1) * 128],
            ctx.Kfm[:, h_ * 128:(h_ + 1) * 128],
            ctx.Kfm[:, h_ * 128:(h_ + 1) * 128], start=True, stop=True))(h),
            reads=[ctx.Kfm[:, h * 128:(h + 1) * 128]],
            writes=[PS[4][:, h * 128:(h + 1) * 128]], tag=f"kk{j}.{h}")
        s.op("pe", (lambda h_: lambda eng: eng.matmul(
            PS[5][:, h_ * 128:(h_ + 1) * 128],
            ctx.Kfm[:, h_ * 128:(h_ + 1) * 128],
            ctx.Qs[:, h_ * 128:(h_ + 1) * 128], start=True, stop=True))(h),
            reads=[ctx.Kfm[:, h * 128:(h + 1) * 128],
                   ctx.Qs[:, h * 128:(h + 1) * 128]],
            writes=[PS[5][:, h * 128:(h + 1) * 128]], tag=f"kq{j}.{h}")
    for h in range(H):
        s.op("dve", (lambda h_: lambda eng: eng.tensor_mul(
            ctx.AqkT[:, h_ * 128:(h_ + 1) * 128],
            PS[5][:, h_ * 128:(h_ + 1) * 128], ctx.mask_iT[:, :]))(h),
            reads=[PS[5][:, h * 128:(h + 1) * 128], ctx.mask_iT[:, :]],
            writes=[ctx.AqkT[:, h * 128:(h + 1) * 128]], tag=f"aqk{j}.{h}")
    for h in range(H):
        s.op("dve", (lambda h_: lambda eng: eng.tensor_mul(
            ctx.Ltmp[:, h_ * 128:(h_ + 1) * 128],
            PS[4][:, h_ * 128:(h_ + 1) * 128], ctx.mask_s[:, :]))(h),
            reads=[PS[4][:, h * 128:(h + 1) * 128], ctx.mask_s[:, :]],
            writes=[ctx.Ltmp[:, h * 128:(h + 1) * 128]], tag=f"lm{j}.{h}")
        s.op("dve", (lambda h_: lambda eng: eng.tensor_mul(
            ctx.LTtmp[:, h_ * 128:(h_ + 1) * 128],
            PS[4][:, h_ * 128:(h_ + 1) * 128], ctx.mask_sT[:, :]))(h),
            reads=[PS[4][:, h * 128:(h + 1) * 128], ctx.mask_sT[:, :]],
            writes=[ctx.LTtmp[:, h * 128:(h + 1) * 128]], tag=f"lmt{j}.{h}")
        s.op("dve", (lambda h_: lambda eng: eng.tensor_scalar(
            ctx.A_b[0][:, h_ * 128:(h_ + 1) * 128],
            ctx.Ltmp[:, h_ * 128:(h_ + 1) * 128],
            ctx.btok[:, h_:h_ + 1], -1.0, OP.mult, OP.mult))(h),
            reads=[ctx.Ltmp[:, h * 128:(h + 1) * 128], ctx.btok[:, h:h + 1]],
            writes=[ctx.A_b[0][:, h * 128:(h + 1) * 128]], tag=f"a0{j}.{h}")
        s.op("dve", (lambda h_: lambda eng: eng.tensor_mul(
            ctx.AT_b[0][:, h_ * 128:(h_ + 1) * 128],
            ctx.LTtmp[:, h_ * 128:(h_ + 1) * 128],
            PS[1][:, h_ * 128:(h_ + 1) * 128]))(h),
            reads=[ctx.LTtmp[:, h * 128:(h + 1) * 128],
                   PS[1][:, h * 128:(h + 1) * 128]],
            writes=[ctx.AT_b[0][:, h * 128:(h + 1) * 128]], tag=f"a0t{j}.{h}")
        s.op("dve", (lambda h_: lambda eng: eng.tensor_add(
            ctx.Eb[0][:, h_ * 128:(h_ + 1) * 128],
            ctx.AT_b[0][:, h_ * 128:(h_ + 1) * 128], ctx.id_bf[:, :]))(h),
            reads=[ctx.AT_b[0][:, h * 128:(h + 1) * 128], ctx.id_bf[:, :]],
            writes=[ctx.Eb[0][:, h * 128:(h + 1) * 128]], tag=f"e0{j}.{h}")

    if upto == "lbuild":
        return
    for lv in range(1, 7):
        src_a, src_t = ctx.A_b[(lv - 1) % 2], ctx.AT_b[(lv - 1) % 2]
        dst_a, dst_t = ctx.A_b[lv % 2], ctx.AT_b[lv % 2]
        src_e, dst_e = ctx.Eb[(lv - 1) % 2], ctx.Eb[lv % 2]
        for h in range(H):
            sl = slice(h * 128, (h + 1) * 128)
            s.op("pe", (lambda a_, t_, h_: lambda eng: eng.matmul(
                PS[4][:, h_ * 128:(h_ + 1) * 128], t_[:, h_ * 128:(h_ + 1) * 128],
                a_[:, h_ * 128:(h_ + 1) * 128], start=True, stop=True))
                (src_a, src_t, h),
                reads=[src_a[:, sl], src_t[:, sl]],
                writes=[PS[4][:, sl]], tag=f"da{j}.{lv}.{h}")
            if lv < 6:
                s.op("pe", (lambda a_, t_, h_: lambda eng: eng.matmul(
                    PS[5][:, h_ * 128:(h_ + 1) * 128],
                    a_[:, h_ * 128:(h_ + 1) * 128],
                    t_[:, h_ * 128:(h_ + 1) * 128], start=True, stop=True))
                    (src_a, src_t, h),
                    reads=[src_a[:, sl], src_t[:, sl]],
                    writes=[PS[5][:, sl]], tag=f"dat{j}.{lv}.{h}")
        s.op("dve", (lambda d_: lambda eng: eng.tensor_copy(d_[:, :], PS[4][:, :]))
             (dst_a), reads=[PS[4][:, :]], writes=[dst_a[:, :]],
             tag=f"dae{j}.{lv}")
        if lv < 6:
            s.op("dve", (lambda d_: lambda eng: eng.tensor_copy(d_[:, :], PS[5][:, :]))
                 (dst_t), reads=[PS[5][:, :]], writes=[dst_t[:, :]],
                 tag=f"date{j}.{lv}")
        for h in range(H):
            sl = slice(h * 128, (h + 1) * 128)
            s.op("pe", (lambda a_, e_, h_: lambda eng: eng.matmul(
                PS[0][:, h_ * 128:(h_ + 1) * 128],
                a_[:, h_ * 128:(h_ + 1) * 128],
                e_[:, h_ * 128:(h_ + 1) * 128], start=True, stop=True))
                (dst_a, src_e, h),
                reads=[dst_a[:, sl], src_e[:, sl]],
                writes=[PS[0][:, sl]], tag=f"de{j}.{lv}.{h}")
        s.op("dve", (lambda se_, de_: lambda eng: eng.tensor_add(
            de_[:, :], se_[:, :], PS[0][:, :]))(src_e, dst_e),
            reads=[src_e[:, :], PS[0][:, :]], writes=[dst_e[:, :]],
            tag=f"dee{j}.{lv}")
    MinvT = ctx.Eb[0]
    if upto == "doub":
        return

    psmap = [PS[2], PS[3], PS[4], PS[5]]
    for h in range(H):
        s.op("pe", (lambda h_, p_: lambda eng: eng.matmul(
            p_[:, 0:384], MinvT[:, h_ * 128:(h_ + 1) * 128],
            ctx.Rb[:, h_ * 384:(h_ + 1) * 384], start=True, stop=True))(h, psmap[h]),
            reads=[MinvT[:, h * 128:(h + 1) * 128],
                   ctx.Rb[:, h * 384:(h + 1) * 384]],
            writes=[psmap[h][:, 0:384]], tag=f"uw{j}.{h}")
        s.op("dve", (lambda h_, p_: lambda eng: eng.tensor_copy(
            ctx.U0W[:, h_ * 384:(h_ + 1) * 384], p_[:, 0:384]))(h, psmap[h]),
            reads=[psmap[h][:, 0:384]],
            writes=[ctx.U0W[:, h * 384:(h + 1) * 384]], tag=f"uwe{j}.{h}")

    if upto == "uw":
        return
    for h in range(H):
        W_h = ctx.U0W[:, h * 384 + 256:(h + 1) * 384]
        s.op("pe", (lambda h_, w_: lambda eng: eng.matmul(
            PS[6][:, h_ * 128:(h_ + 1) * 128], w_,
            ctx.Ktok[:, h_ * 128:(h_ + 1) * 128], start=True, stop=True))(h, W_h),
            reads=[W_h, ctx.Ktok[:, h * 128:(h + 1) * 128]],
            writes=[PS[6][:, h * 128:(h + 1) * 128]], tag=f"apt{j}.{h}")
    s.op("act", lambda eng: eng.mul(ctx.nApT[:, :], PS[6][:, :], -1.0),
         reads=[PS[6][:, :]], writes=[ctx.nApT[:, :]], tag=f"apte{j}")
    for h in range(H):
        W_h = ctx.U0W[:, h * 384 + 256:(h + 1) * 384]
        s.op("pe", (lambda h_, w_: lambda eng: eng.matmul(
            PS[0][:, h_ * 128:(h_ + 1) * 128], w_,
            ctx.AqkT[:, h_ * 128:(h_ + 1) * 128], start=True, stop=True))(h, W_h),
            reads=[W_h, ctx.AqkT[:, h * 128:(h + 1) * 128]],
            writes=[PS[0][:, h * 128:(h + 1) * 128]], tag=f"awt{j}.{h}")
    s.op("dve", lambda eng: eng.tensor_sub(ctx.GT[:, :], ctx.Qs[:, :], PS[0][:, :]),
         reads=[ctx.Qs[:, :], PS[0][:, :]], writes=[ctx.GT[:, :]], tag=f"gt{j}")

    for h in range(H):
        s.op("pe", (lambda h_: lambda eng: eng.matmul(
            PS[6][:, h_ * 128:(h_ + 1) * 128],
            ctx.Pa_sh[:, h_ * 128:(h_ + 1) * 128],
            ctx.GT[:, h_ * 128:(h_ + 1) * 128], start=True, stop=True))(h),
            reads=[ctx.Pa_sh[:, h * 128:(h + 1) * 128],
                   ctx.GT[:, h * 128:(h + 1) * 128]],
            writes=[PS[6][:, h * 128:(h + 1) * 128]], tag=f"gpt{j}.{h}")
    s.op("dve", lambda eng: eng.tensor_copy(ctx.GPTb[:, :], PS[6][:, :]),
         reads=[PS[6][:, :]], writes=[ctx.GPTb[:, :]], tag=f"gpte{j}")
    s.op("sp", (lambda j_: lambda eng: eng.dma_start(
        out=ctx.cstore_out[j_, :, 0:512], in_=ctx.GPTb[:, :]))(j),
        reads=[ctx.GPTb[:, :]], dma_sem="cso", force_inc=True, tag=f"csg{j}")

    for h in range(H):
        ps = PS[2] if h < 2 else PS[3]
        off = (h % 2) * 256
        U0_h = ctx.U0W[:, h * 384: h * 384 + 256]
        s.op("pe", (lambda h_, p_, o_, u_: lambda eng: [
            eng.matmul(p_[:, o_:o_ + 256],
                       ctx.AqkT[:, h_ * 128:(h_ + 1) * 128], u_,
                       start=True, stop=False),
            eng.matmul(p_[:, o_:o_ + 256],
                       ctx.GT[:, h_ * 128:(h_ + 1) * 128],
                       ctx.Tacc_sh[:, h_ * 256:(h_ + 1) * 256],
                       start=False, stop=True)][-1])(h, ps, off, U0_h),
            reads=[ctx.AqkT[:, h * 128:(h + 1) * 128], U0_h,
                   ctx.GT[:, h * 128:(h + 1) * 128],
                   ctx.Tacc_sh[:, h * 256:(h + 1) * 256]],
            writes=[ps[:, off:off + 256]], tag=f"o0{j}.{h}")
    s.op("dve", lambda eng: eng.tensor_copy(ctx.O0p[:, 0:512], PS[2][:, :]),
         reads=[PS[2][:, :]], writes=[ctx.O0p[:, 0:512]], tag=f"o0e0{j}")
    s.op("dve", lambda eng: eng.tensor_copy(ctx.O0p[:, 512:1024], PS[3][:, :]),
         reads=[PS[3][:, :]], writes=[ctx.O0p[:, 512:1024]], tag=f"o0e1{j}")
    s.op("sp", (lambda j_: lambda eng: eng.dma_start(
        out=ctx.cstore_out[j_, :, 512:1536], in_=ctx.O0p[:, :]))(j),
        reads=[ctx.O0p[:, :]], dma_sem="cso", force_inc=True, tag=f"cso{j}")

    if upto == "store":
        return
    for h in range(H):
        ps = PS[4] if h < 2 else PS[5]
        off = (h % 2) * 256
        U0_h = ctx.U0W[:, h * 384: h * 384 + 256]
        s.op("pe", (lambda h_, p_, o_, u_: lambda eng: [
            eng.matmul(p_[:, o_:o_ + 256],
                       ctx.Ktok[:, h_ * 128:(h_ + 1) * 128], u_,
                       start=True, stop=False),
            eng.matmul(p_[:, o_:o_ + 256],
                       ctx.nApT[:, h_ * 128:(h_ + 1) * 128],
                       ctx.Tacc_sh[:, h_ * 256:(h_ + 1) * 256],
                       start=False, stop=True)][-1])(h, ps, off, U0_h),
            reads=[ctx.Ktok[:, h * 128:(h + 1) * 128], U0_h,
                   ctx.nApT[:, h * 128:(h + 1) * 128],
                   ctx.Tacc_sh[:, h * 256:(h + 1) * 256]],
            writes=[ps[:, off:off + 256]], tag=f"tc{j}.{h}")
    s.op("dve", lambda eng: eng.tensor_add(
        ctx.Tacc[:, 0:512], ctx.Tacc[:, 0:512], PS[4][:, :]),
        reads=[ctx.Tacc[:, 0:512], PS[4][:, :]], writes=[ctx.Tacc[:, 0:512]],
        tag=f"tu0{j}")
    s.op("dve", lambda eng: eng.tensor_add(
        ctx.Tacc[:, 512:1024], ctx.Tacc[:, 512:1024], PS[5][:, :]),
        reads=[ctx.Tacc[:, 512:1024], PS[5][:, :]],
        writes=[ctx.Tacc[:, 512:1024]], tag=f"tu1{j}")
    for h in range(H):
        s.op("pe", (lambda h_: lambda eng: eng.matmul(
            PS[0][:, h_ * 128:(h_ + 1) * 128],
            ctx.nApT[:, h_ * 128:(h_ + 1) * 128],
            ctx.Pa_sh[:, h_ * 128:(h_ + 1) * 128], start=True, stop=True))(h),
            reads=[ctx.nApT[:, h * 128:(h + 1) * 128],
                   ctx.Pa_sh[:, h * 128:(h + 1) * 128]],
            writes=[PS[0][:, h * 128:(h + 1) * 128]], tag=f"pu{j}.{h}")
    s.op("dve", lambda eng: eng.tensor_add(
        ctx.Pa[:, :], ctx.Pa[:, :], PS[0][:, :]),
        reads=[ctx.Pa[:, :], PS[0][:, :]], writes=[ctx.Pa[:, :]], tag=f"pua{j}")
    s.op("dve", lambda eng: eng.tensor_copy(ctx.Tacc_sh[:, :], ctx.Tacc[:, :]),
         reads=[ctx.Tacc[:, :]], writes=[ctx.Tacc_sh[:, :]], tag=f"tsh{j}")
    s.op("dve", lambda eng: eng.tensor_copy(ctx.Pa_sh[:, :], ctx.Pa[:, :]),
         reads=[ctx.Pa[:, :]], writes=[ctx.Pa_sh[:, :]], tag=f"psh{j}")


def emit_phase_a_export(ctx, s):
    for h in range(H):
        s.op("pe", (lambda h_: lambda eng: eng.transpose(
            ctx.PS[0][:, 0:128], ctx.Pa[:, h_ * 128:(h_ + 1) * 128],
            ctx.id_f32[:, :]))(h),
            reads=[ctx.Pa[:, h * 128:(h + 1) * 128], ctx.id_f32[:, :]],
            writes=[ctx.PS[0][:, 0:128]], tag=f"paT{h}")
        dst = ctx.paT_sb[:, h * 128:(h + 1) * 128]
        s.op("dve", (lambda d_: lambda eng: eng.tensor_copy(d_, ctx.PS[0][:, 0:128]))
             (dst), reads=[ctx.PS[0][:, 0:128]], writes=[dst], tag=f"paTe{h}")
    s.op("sp", lambda eng: eng.dma_start(out=ctx.paT_out, in_=ctx.paT_sb[:, :]),
         reads=[ctx.paT_sb[:, :]], dma_sem="mapo", force_inc=True, tag="paTo")
    s.op("sp", lambda eng: eng.dma_start(out=ctx.ta_out, in_=ctx.Tacc[:, :]),
         reads=[ctx.Tacc[:, :]], dma_sem="mapo", force_inc=True, tag="tao")


def build_l1(stage="full", nchunks=NCHUNK, upto="full"):
    nc = bass.Bass("TRN2", target_bir_lowering=False)
    import contextlib
    stack = contextlib.ExitStack()
    ctx = Ctx()
    ctx.nc = nc
    dram_io_l1(nc, ctx)
    if stage in ("ln",):
        ctx.dbg_xn = nc.dram_tensor(
            "dbg_xn", [G512, 128, 4096], BF16, kind="ExternalOutput").ap()
    alloc_common(nc, ctx, stack)
    psum_alloc(nc, ctx, stack)

    import contextlib as _ctl
    sb = lambda name, shape, dt: stack.enter_context(
        nc.sbuf_tensor(name, shape, dt))
    if stage != "ln":
        ctx.router_bf = sb("router_bf", [128, 128], BF16)
        ctx.negmax = sb("negmax", [128, 1], F32)
        ctx.ssum = sb("ssum", [128, 1], F32)
        ctx.rsum = sb("rsum", [128, 1], F32)
        ctx.probs = sb("probs", [128, 16], F32)
        ctx.ph = sb("ph", [128, 16], F32)
        ctx.pz = sb("pz", [128, 16], F32)
        ctx.m8 = sb("m8", [128, 8], F32)
        ctx.gt = sb("gt", [128, 16], F32)
        ctx.gate_em = sb("gate_em", [16, TOK], BF16)

    s = Sched(nc)
    ctx.sched = s
    s.psum_ids = {id(t) for t in ctx.psum_tensors}
    load_constants(ctx, s)
    if stage != "ln":
        s.op("pool", lambda eng: eng.dma_start(
            out=ctx.router_bf[:, :].rearrange("p (k r) -> p k r", k=8),
            in_=ctx.w_router.rearrange("(k p) r -> p k r", p=128)),
            writes=[ctx.router_bf[:, :]], dma_sem="cst", tag="wrouter")
    xtok_stack = _ctl.ExitStack()
    ctx.xtok = [xtok_stack.enter_context(
        nc.sbuf_tensor(f"xtok{i}", [128, 1024], F32)) for i in range(2)]
    emit_load_x(ctx, s)
    xtok_stack.close()
    s.barrier("free_xtok")
    ln_stack = _ctl.ExitStack()
    ctx.xn_w = ln_stack.enter_context(nc.sbuf_tensor("xn_w", [128, 4096], BF16))
    for gi in range(G512):
        emit_ln(ctx, s, gi,
                dst_hbm=(ctx.dbg_xn if stage == "ln" else ctx.xn_scr))
        if stage != "ln":
            emit_router_gates(ctx, s, gi, tag="f1")
    if stage != "ln":
        ln_stack.close()
        s.barrier("free_ln")
    if stage == "ln":
        s.op("sp", lambda eng: eng.dma_start(out=ctx.x1_out, in_=ctx.X[:, :]),
             reads=[ctx.X[:, :]], dma_sem="xout", force_inc=True, tag="xdump")
    elif stage in ("ffn1", "l1"):
        ffn_stack = _ctl.ExitStack()
        alloc_ffn(nc, ctx, ffn_stack)
        emit_ffn_pass(ctx, s, ctx.xn_scr, tag="f1")
        ffn_stack.close()
        s.barrier("free_ffn")
        if stage == "l1":
            attn_stack = _ctl.ExitStack()
            alloc_attn(nc, ctx, attn_stack)
            ctx.xn_w = attn_stack.enter_context(
                nc.sbuf_tensor("xa_w", [128, 4096], BF16))
            emit_attn_wload(ctx, s)
            emit_phase_a_init(ctx, s)
            for gi in range(G512):
                emit_ln(ctx, s, gi, dst_hbm=None)
                for jj in range(4):
                    if gi * 4 + jj < nchunks:
                        emit_chunk(ctx, s, gi * 4 + jj, upto=upto)
            emit_phase_a_export(ctx, s)
        s.op("sp", lambda eng: eng.dma_start(out=ctx.x1_out, in_=ctx.X[:, :]),
             reads=[ctx.X[:, :]], dma_sem="xout", force_inc=True, tag="xdump")
    else:
        raise NotImplementedError(stage)

    s.barrier("end_join")
    s.finalize()
    sems = {}
    for name in s.sem_names():
        sems[name] = stack.enter_context(nc.semaphore(name))
    with nc.Block() as block:
        @block.tensor
        def _(eng):
            s.emit_engine("pe", eng, sems)

        @block.scalar
        def _(eng):
            s.emit_engine("act", eng, sems)

        @block.vector
        def _(eng):
            s.emit_engine("dve", eng, sems)

        @block.gpsimd
        def _(eng):
            s.emit_engine("pool", eng, sems)

        @block.sync
        def _(eng):
            s.emit_engine("sp", eng, sems)
    ctx.stack = stack
    return nc, ctx


def dram_io_l2(nc, ctx):
    d = lambda n, s, dt, k: nc.dram_tensor(n, s, dt, kind=k).ap()
    ctx.x1_in = d("x1_in", [128, 8 * TOK], F32, "ExternalInput")
    ctx.gsilu_in = d("gsilu_in", [NCHUNK, 128, 1024], BF16, "ExternalInput")
    ctx.cstore_in = d("cstore_in", [NCHUNK, 128, 1536], BF16, "ExternalInput")
    ctx.maps_paT = d("maps_paT", [3, 128, 512], BF16, "ExternalInput")
    ctx.maps_ta = d("maps_ta", [3, 128, 1024], F32, "ExternalInput")
    ctx.w_upT = d("w_upT", [NE, D, E], F32, "ExternalInput")
    ctx.w_dnT = d("w_dnT", [NE, E, D], F32, "ExternalInput")
    ctx.w_router = d("w_router", [D, 16], F32, "ExternalInput")
    ctx.w_o = d("w_o", [D, D], F32, "ExternalInput")
    ctx.w_normw = d("w_normw", [1, 256], F32, "ExternalInput")
    ctx.c_idf = d("c_idf", [128, 128], F32, "ExternalInput")
    ctx.c_idb = d("c_idb", [128, 128], BF16, "ExternalInput")
    ctx.c_mask_s = d("c_mask_s", [128, 128], BF16, "ExternalInput")
    ctx.c_mask_sT = d("c_mask_sT", [128, 128], BF16, "ExternalInput")
    ctx.c_mask_iT = d("c_mask_iT", [128, 128], BF16, "ExternalInput")
    ctx.c_ones_rf = d("c_ones_rf", [1, 128], F32, "ExternalInput")
    ctx.c_ones_rb = d("c_ones_rb", [1, 128], BF16, "ExternalInput")
    ctx.c_nones_rf = d("c_nones_rf", [1, 128], F32, "ExternalInput")
    ctx.c_ones_cb = d("c_ones_cb", [128, 1], BF16, "ExternalInput")
    ctx.y_out = d("y_out", [TOK, D], F32, "ExternalOutput")
    ctx.xn_scr = nc.dram_tensor("xn2_scr", [G512, 128, 4096], BF16).ap()


def emit_load_x1(ctx, s):
    s.op("sp", lambda eng: eng.dma_start(out=ctx.X[:, :], in_=ctx.x1_in),
         reads=[ctx.x1_in], writes=[ctx.X[:, :]], dma_sem="x1l", tag="x1l")


def emit_phase_b(ctx, s):
    PS = ctx.PS
    s.op("pool", lambda eng: eng.dma_start(
        out=ctx.nw_row[0:1, :], in_=ctx.w_normw),
        reads=[ctx.w_normw], writes=[ctx.nw_row[0:1, :]], dma_sem="nw",
        tag="nwl")
    s.op("pe", lambda eng: eng.matmul(
        PS[6][:, 0:256], ctx.ones_row_f[:, :], ctx.nw_row[0:1, :],
        start=True, stop=True),
        reads=[ctx.ones_row_f[:, :], ctx.nw_row[0:1, :]],
        writes=[PS[6][:, 0:256]], tag="nwb")
    s.op("dve", lambda eng: eng.tensor_copy(ctx.nw_bc[:, :], PS[6][:, 0:256]),
         reads=[PS[6][:, 0:256]], writes=[ctx.nw_bc[:, :]], tag="nwe")
    s.op("pool", lambda eng: eng.dma_start(
        out=ctx.wo_bf[:, :].rearrange("p (k n) -> p k n", k=8),
        in_=ctx.w_o.rearrange("(k p) n -> p k n", p=128)),
        reads=[ctx.w_o], writes=[ctx.wo_bf[:, :]], dma_sem="wo", tag="wol")
    s.op("dve", lambda eng: eng.memset(ctx.S0c[:, :], 0.0),
         writes=[ctx.S0c[:, :]], tag="s0z")
    s.op("dve", lambda eng: eng.memset(ctx.S0c_sh[:, :], 0.0),
         writes=[ctx.S0c_sh[:, :]], tag="s0zs")
    for jp in range(3):
        s.op("pool", (lambda j_: lambda eng: eng.dma_start(
            out=ctx.mp_paT[:, :], in_=ctx.maps_paT[j_]))(jp),
            reads=[ctx.maps_paT[jp]], writes=[ctx.mp_paT[:, :]],
            dma_sem="mp", tag=f"mpl{jp}")
        s.op("pool", (lambda j_: lambda eng: eng.dma_start(
            out=ctx.mp_ta[:, :], in_=ctx.maps_ta[j_]))(jp),
            reads=[ctx.maps_ta[jp]], writes=[ctx.mp_ta[:, :]],
            dma_sem="mp", tag=f"mtl{jp}")
        for h in range(H):
            ps = PS[0] if h < 2 else PS[1]
            off = (h % 2) * 256
            s.op("pe", (lambda h_, p_, o_: lambda eng: eng.matmul(
                p_[:, o_:o_ + 256], ctx.mp_paT[:, h_ * 128:(h_ + 1) * 128],
                ctx.S0c_sh[:, h_ * 256:(h_ + 1) * 256],
                start=True, stop=True))(h, ps, off),
                reads=[ctx.mp_paT[:, h * 128:(h + 1) * 128],
                       ctx.S0c_sh[:, h * 256:(h + 1) * 256]],
                writes=[ps[:, off:off + 256]], tag=f"cmp{jp}.{h}")
        s.op("dve", lambda eng: eng.tensor_add(
            ctx.S0c[:, 0:512], PS[0][:, :], ctx.mp_ta[:, 0:512]),
            reads=[PS[0][:, :], ctx.mp_ta[:, 0:512]],
            writes=[ctx.S0c[:, 0:512]], tag=f"cadd0{jp}")
        s.op("dve", lambda eng: eng.tensor_add(
            ctx.S0c[:, 512:1024], PS[1][:, :], ctx.mp_ta[:, 512:1024]),
            reads=[PS[1][:, :], ctx.mp_ta[:, 512:1024]],
            writes=[ctx.S0c[:, 512:1024]], tag=f"cadd1{jp}")
        s.op("dve", lambda eng: eng.tensor_copy(ctx.S0c_sh[:, :], ctx.S0c[:, :]),
             reads=[ctx.S0c[:, :]], writes=[ctx.S0c_sh[:, :]], tag=f"csh{jp}")
    for j in range(NCHUNK):
        csb = ctx.cs_buf[j % 2]
        gsb = ctx.gs_buf[j % 2]
        s.op("sp", (lambda j_, b_: lambda eng: eng.dma_start(
            out=b_[:, :], in_=ctx.cstore_in[j_]))(j, csb),
            reads=[ctx.cstore_in[j]], writes=[csb[:, :]], dma_sem="csl",
            tag=f"csl{j}")
        s.op("sp", (lambda j_, b_: lambda eng: eng.dma_start(
            out=b_[:, :], in_=ctx.gsilu_in[j_]))(j, gsb),
            reads=[ctx.gsilu_in[j]], writes=[gsb[:, :]], dma_sem="gsl",
            tag=f"gsl{j}")
        for h in range(H):
            ps = PS[2] if h < 2 else PS[3]
            off = (h % 2) * 256
            s.op("pe", (lambda h_, p_, o_, b_: lambda eng: eng.matmul(
                p_[:, o_:o_ + 256], b_[:, h_ * 128:(h_ + 1) * 128],
                ctx.S0c_sh[:, h_ * 256:(h_ + 1) * 256],
                start=True, stop=True))(h, ps, off, csb),
                reads=[csb[:, h * 128:(h + 1) * 128],
                       ctx.S0c_sh[:, h * 256:(h + 1) * 256]],
                writes=[ps[:, off:off + 256]], tag=f"ob{j}.{h}")
        s.op("dve", (lambda b_: lambda eng: eng.tensor_add(
            ctx.o_sb[:, 0:512], PS[2][:, :], b_[:, 512:1024]))(csb),
            reads=[PS[2][:, :], csb[:, 512:1024]],
            writes=[ctx.o_sb[:, 0:512]], tag=f"oa0{j}")
        s.op("dve", (lambda b_: lambda eng: eng.tensor_add(
            ctx.o_sb[:, 512:1024], PS[3][:, :], b_[:, 1024:1536]))(csb),
            reads=[PS[3][:, :], csb[:, 1024:1536]],
            writes=[ctx.o_sb[:, 512:1024]], tag=f"oa1{j}")
        for h in range(H):
            s.op("act", (lambda h_: lambda eng: eng.activation(
                ctx.osq[:, :], ctx.o_sb[:, h_ * 256:(h_ + 1) * 256], AF.Square,
                accum_out=ctx.rms_c[:, h_:h_ + 1]))(h),
                reads=[ctx.o_sb[:, h * 256:(h + 1) * 256]],
                writes=[ctx.osq[:, :], ctx.rms_c[:, h:h + 1]], tag=f"or{j}.{h}")
        s.op("act", lambda eng: eng.activation(
            ctx.rms_c[:, 4:8], ctx.rms_c[:, 0:4], AF.Sqrt,
            bias=ctx.eps5[:, 0:1], scale=1.0 / DV),
            reads=[ctx.rms_c[:, 0:4], ctx.eps5[:, 0:1]],
            writes=[ctx.rms_c[:, 4:8]], tag=f"os{j}")
        s.op("dve", lambda eng: eng.reciprocal(ctx.rms_c[:, 8:12], ctx.rms_c[:, 4:8]),
             reads=[ctx.rms_c[:, 4:8]], writes=[ctx.rms_c[:, 8:12]],
             tag=f"orc{j}")
        for h in range(H):
            s.op("dve", (lambda h_: lambda eng: eng.tensor_scalar(
                ctx.o_sb[:, h_ * 256:(h_ + 1) * 256],
                ctx.o_sb[:, h_ * 256:(h_ + 1) * 256],
                ctx.rms_c[:, 8 + h_:9 + h_], None, OP.mult))(h),
                reads=[ctx.o_sb[:, h * 256:(h + 1) * 256],
                       ctx.rms_c[:, 8 + h:9 + h]],
                writes=[ctx.o_sb[:, h * 256:(h + 1) * 256]], tag=f"on{j}.{h}")
            s.op("dve", (lambda h_: lambda eng: eng.tensor_mul(
                ctx.o_sb[:, h_ * 256:(h_ + 1) * 256],
                ctx.o_sb[:, h_ * 256:(h_ + 1) * 256], ctx.nw_bc[:, :]))(h),
                reads=[ctx.o_sb[:, h * 256:(h + 1) * 256], ctx.nw_bc[:, :]],
                writes=[ctx.o_sb[:, h * 256:(h + 1) * 256]], tag=f"onw{j}.{h}")
        s.op("dve", (lambda b_: lambda eng: eng.tensor_mul(
            ctx.og_bf[:, :], ctx.o_sb[:, :], b_[:, :]))(gsb),
            reads=[ctx.o_sb[:, :], gsb[:, :]], writes=[ctx.og_bf[:, :]],
            tag=f"og{j}")
        for c2 in range(8):
            s.op("pe", (lambda c_: lambda eng: eng.transpose(
                ctx.PSB[:, c_ * 128:(c_ + 1) * 128],
                ctx.og_bf[:, c_ * 128:(c_ + 1) * 128], ctx.id_bf[:, :]))(c2),
                reads=[ctx.og_bf[:, c2 * 128:(c2 + 1) * 128], ctx.id_bf[:, :]],
                writes=[ctx.PSB[:, c2 * 128:(c2 + 1) * 128]], tag=f"ot{j}.{c2}")
        jj = j % 4
        for half in range(2):
            dst = [ctx.og_fm[:, (half * 4 + q) * 512 + jj * 128:
                             (half * 4 + q) * 512 + (jj + 1) * 128]
                   for q in range(4)]

            def mk_ev(dsts_, half_):
                def f(eng):
                    for q in range(4):
                        inst = eng.tensor_copy(
                            dsts_[q],
                            ctx.PSB[:, half_ * 512 + q * 128:
                                    half_ * 512 + (q + 1) * 128])
                    return inst
                return f
            s.op("dve", mk_ev(dst, half),
                 reads=[ctx.PSB[:, half * 512:(half + 1) * 512]],
                 writes=dst, tag=f"ote{j}.{half}")
        if jj == 3:
            gi = j // 4
            for m in range(8):
                ps = PS[4 + (m % 2)]
                stats = [ctx.wo_bf[:, k * 1024 + m * 128: k * 1024 + (m + 1) * 128]
                         for k in range(8)]
                movs = [ctx.og_fm[:, k * 512:(k + 1) * 512] for k in range(8)]

                def mk_wo(stats_, movs_, ps_):
                    def f(eng):
                        for i in range(8):
                            inst = eng.matmul(ps_[:, :], stats_[i], movs_[i],
                                              start=(i == 0), stop=(i == 7))
                        return inst
                    return f
                s.op("pe", mk_wo(stats, movs, ps),
                     reads=stats + movs, writes=[ps[:, :]], tag=f"wo{gi}.{m}")
                xv = ctx.X[:, m * TOK + gi * 512: m * TOK + (gi + 1) * 512]
                s.op("dve", (lambda xv_, ps_: lambda eng: eng.tensor_add(
                    xv_, xv_, ps_[:, :]))(xv, ps),
                    reads=[xv, ps[:, :]], writes=[xv], tag=f"woa{gi}.{m}")


def emit_output(ctx, s):
    for j in range(NCHUNK):
        for k in range(8):
            ps = ctx.PS[k // 4]
            s.op("pe", (lambda k_, p_, j_: lambda eng: eng.transpose(
                p_[:, (k_ % 4) * 128:(k_ % 4 + 1) * 128],
                ctx.X[:, k_ * TOK + j_ * 128: k_ * TOK + (j_ + 1) * 128],
                ctx.id_f32[:, :]))(k, ps, j),
                reads=[ctx.X[:, k * TOK + j * 128: k * TOK + (j + 1) * 128],
                       ctx.id_f32[:, :]],
                writes=[ps[:, (k % 4) * 128:(k % 4 + 1) * 128]],
                tag=f"yT{j}.{k}")
        for half in range(2):
            s.op("dve", (lambda h_, j_: lambda eng: eng.tensor_copy(
                ctx.ot_sb[:, h_ * 512:(h_ + 1) * 512], ctx.PS[h_][:, :]))
                (half, j),
                reads=[ctx.PS[half][:, :]],
                writes=[ctx.ot_sb[:, half * 512:(half + 1) * 512]],
                tag=f"yTe{j}.{half}")
        s.op("sp", (lambda j_: lambda eng: eng.dma_start(
            out=ctx.y_out[j_ * 128:(j_ + 1) * 128, :], in_=ctx.ot_sb[:, :]))(j),
            reads=[ctx.ot_sb[:, :]], writes=[ctx.y_out[j * 128:(j + 1) * 128, :]],
            dma_sem="yo", force_inc=True, tag=f"yo{j}")


def build_l2(stage="full"):
    nc = bass.Bass("TRN2", target_bir_lowering=False)
    import contextlib as _ctl
    stack = _ctl.ExitStack()
    ctx = Ctx()
    ctx.nc = nc
    dram_io_l2(nc, ctx)
    alloc_common(nc, ctx, stack)
    psum_alloc(nc, ctx, stack)
    sb = lambda name, shape, dt: stack.enter_context(
        nc.sbuf_tensor(name, shape, dt))
    ctx.router_bf = sb("router_bf", [128, 128], BF16)
    ctx.negmax = sb("negmax", [128, 1], F32)
    ctx.ssum = sb("ssum", [128, 1], F32)
    ctx.rsum = sb("rsum", [128, 1], F32)
    ctx.probs = sb("probs", [128, 16], F32)
    ctx.ph = sb("ph", [128, 16], F32)
    ctx.pz = sb("pz", [128, 16], F32)
    ctx.m8 = sb("m8", [128, 8], F32)
    ctx.gt = sb("gt", [128, 16], F32)
    ctx.gate_em = sb("gate_em", [16, TOK], BF16)

    s = Sched(nc)
    ctx.sched = s
    s.psum_ids = {id(t) for t in ctx.psum_tensors}
    load_constants(ctx, s)
    s.op("pool", lambda eng: eng.dma_start(
        out=ctx.router_bf[:, :].rearrange("p (k r) -> p k r", k=8),
        in_=ctx.w_router.rearrange("(k p) r -> p k r", p=128)),
        writes=[ctx.router_bf[:, :]], dma_sem="cst", tag="wrouter")
    emit_load_x1(ctx, s)
    pb_stack = _ctl.ExitStack()
    sb2 = lambda name, shape, dt: pb_stack.enter_context(
        nc.sbuf_tensor(name, shape, dt))
    ctx.wo_bf = sb2("wo_bf", [128, 8 * 1024], BF16)
    ctx.nw_row = sb2("nw_row", [1, 256], F32)
    ctx.nw_bc = sb2("nw_bc", [128, 256], F32)
    ctx.S0c = sb2("S0c", [128, 1024], F32)
    ctx.S0c_sh = sb2("S0c_sh", [128, 1024], BF16)
    ctx.mp_paT = sb2("mp_paT", [128, 512], BF16)
    ctx.mp_ta = sb2("mp_ta", [128, 1024], F32)
    ctx.cs_buf = [sb2(f"cs_buf{i}", [128, 1536], BF16) for i in range(2)]
    ctx.gs_buf = [sb2(f"gs_buf{i}", [128, 1024], BF16) for i in range(2)]
    ctx.o_sb = sb2("o_sb", [128, 1024], F32)
    ctx.osq = sb2("osq", [128, 256], BF16)
    ctx.rms_c = sb2("rms_c", [128, 12], F32)
    ctx.og_bf = sb2("og_bf", [128, 1024], BF16)
    ctx.og_fm = sb2("og_fm", [128, 8 * 512], BF16)
    emit_phase_b(ctx, s)
    pb_stack.close()
    s.barrier("free_pb")
    if stage == "pb":
        s.op("sp", lambda eng: eng.dma_start(
            out=ctx.y_out.rearrange("(a p) d -> p a d", p=128),
            in_=ctx.X[:, :]),
            reads=[ctx.X[:, :]], writes=[ctx.y_out],
            dma_sem="yo", force_inc=True, tag="xdump2")
    else:
        ln3_stack = _ctl.ExitStack()
        ctx.xn_w = ln3_stack.enter_context(
            nc.sbuf_tensor("xn_w2", [128, 4096], BF16))
        for gi in range(G512):
            emit_ln(ctx, s, gi, dst_hbm=ctx.xn_scr)
            emit_router_gates(ctx, s, gi, tag="f2")
        ln3_stack.close()
        s.barrier("free_ln3")
        ffn_stack = _ctl.ExitStack()
        alloc_ffn(nc, ctx, ffn_stack)
        emit_ffn_pass(ctx, s, ctx.xn_scr, tag="f2")
        ffn_stack.close()
        s.barrier("free_ffn2")
        out_stack = _ctl.ExitStack()
        ctx.ot_sb = out_stack.enter_context(
            nc.sbuf_tensor("ot_sb", [128, 1024], F32))
        emit_output(ctx, s)

    s.barrier("end_join")
    s.finalize()
    sems = {}
    for name in s.sem_names():
        sems[name] = stack.enter_context(nc.semaphore(name))
    with nc.Block() as block:
        @block.tensor
        def _(eng):
            s.emit_engine("pe", eng, sems)

        @block.scalar
        def _(eng):
            s.emit_engine("act", eng, sems)

        @block.vector
        def _(eng):
            s.emit_engine("dve", eng, sems)

        @block.gpsimd
        def _(eng):
            s.emit_engine("pool", eng, sems)

        @block.sync
        def _(eng):
            s.emit_engine("sp", eng, sems)
    ctx.stack = stack
    return nc, ctx


def make_constants():
    c = {}
    c["c_idf"] = np.eye(128, dtype=np.float32)
    c["c_idb"] = np.eye(128, dtype=np.float32).astype(BFNP)
    tril_s = np.tril(np.ones((128, 128), np.float32), -1)
    c["c_mask_s"] = tril_s.astype(BFNP)
    c["c_mask_sT"] = tril_s.T.copy().astype(BFNP)
    c["c_mask_iT"] = np.tril(np.ones((128, 128), np.float32), 0).T.copy().astype(BFNP)
    c["c_ones_rf"] = np.ones((1, 128), np.float32)
    c["c_ones_rb"] = np.ones((1, 128), np.float32).astype(BFNP)
    c["c_nones_rf"] = -np.ones((1, 128), np.float32)
    c["c_ones_cb"] = np.ones((128, 1), np.float32).astype(BFNP)
    return c


def host_inputs(inputs):
    x = np.ascontiguousarray(np.asarray(inputs["L0_attn_x"], np.float32))
    upT = np.ascontiguousarray(np.concatenate(
        [inputs["L0_ffn_shared_up"], inputs["L0_ffn_routed_up"]], 0
    ).transpose(0, 2, 1))
    dnT = np.ascontiguousarray(np.concatenate(
        [inputs["L0_ffn_shared_down"], inputs["L0_ffn_routed_down"]], 0
    ).transpose(0, 2, 1))
    routerT = np.ascontiguousarray(np.asarray(inputs["L0_ffn_router_w"]).T)
    consts = make_constants()
    per_core = []
    for cix in range(N_CORES):
        b, q = divmod(cix, 4)
        m = dict(consts)
        m["x_in"] = np.ascontiguousarray(x[b, q * TOK:(q + 1) * TOK, :])
        m["w_upT"] = upT
        m["w_dnT"] = dnT
        m["w_router"] = routerT
        m["w_q"] = np.asarray(inputs["L0_attn_Wq"], np.float32)
        m["w_k"] = np.asarray(inputs["L0_attn_Wk"], np.float32)
        m["w_v"] = np.asarray(inputs["L0_attn_Wv"], np.float32)
        m["w_b"] = np.asarray(inputs["L0_attn_Wb"], np.float32)
        m["w_g"] = np.asarray(inputs["L0_attn_Wg"], np.float32)
        per_core.append(m)
    return per_core


def host_inputs_l2(inputs, l1_results):
    upT = np.ascontiguousarray(np.concatenate(
        [inputs["L0_ffn_shared_up"], inputs["L0_ffn_routed_up"]], 0
    ).transpose(0, 2, 1))
    dnT = np.ascontiguousarray(np.concatenate(
        [inputs["L0_ffn_shared_down"], inputs["L0_ffn_routed_down"]], 0
    ).transpose(0, 2, 1))
    routerT = np.ascontiguousarray(np.asarray(inputs["L0_ffn_router_w"]).T)
    consts = make_constants()
    paT_id = np.zeros((128, 512), np.float32)
    for h in range(H):
        paT_id[:, h * 128:(h + 1) * 128] = np.eye(128, dtype=np.float32)
    paT_id = paT_id.astype(BFNP)
    ta_zero = np.zeros((128, 1024), np.float32)
    per_core = []
    for cix in range(N_CORES):
        b, q = divmod(cix, 4)
        m = dict(consts)
        r = l1_results[cix]
        m["x1_in"] = np.asarray(r["x1_out"])
        m["gsilu_in"] = np.asarray(r["gsilu_out"])
        m["cstore_in"] = np.asarray(r["cstore_out"])
        preds = [b * 4 + i for i in range(q)]
        pa_l = [paT_id] * (3 - len(preds)) + \
            [np.asarray(l1_results[p]["paT_out"]) for p in preds]
        ta_l = [ta_zero] * (3 - len(preds)) + \
            [np.asarray(l1_results[p]["ta_out"]) for p in preds]
        m["maps_paT"] = np.stack(pa_l)
        m["maps_ta"] = np.stack(ta_l)
        m["w_upT"] = upT
        m["w_dnT"] = dnT
        m["w_router"] = routerT
        m["w_o"] = np.asarray(inputs["L0_attn_Wo"], np.float32)
        m["w_normw"] = np.asarray(inputs["L0_attn_norm_w"],
                                  np.float32).reshape(1, 256)
        per_core.append(m)
    return per_core


_BUILT = {}


def kernel(**inputs):
    per_core = host_inputs(inputs)
    if "l1" not in _BUILT:
        _BUILT["l1"] = build_l1(stage="l1")
        _BUILT["l2"] = build_l2(stage="full")
    nc1, _ = _BUILT["l1"]
    nc2, _ = _BUILT["l2"]
    res1 = run_bass_kernel_spmd(nc1, per_core, core_ids=list(range(N_CORES)))
    per_core2 = host_inputs_l2(inputs, res1.results)
    res2 = run_bass_kernel_spmd(nc2, per_core2, core_ids=list(range(N_CORES)))
    out = np.zeros((2, 8192, 1024), np.float32)
    for cix in range(N_CORES):
        b, q = divmod(cix, 4)
        out[b, q * TOK:(q + 1) * TOK] = res2.results[cix]["y_out"]
    return out
